# version 01
# baseline (speedup 1.0000x reference)
import bisect
import numpy as np
import ml_dtypes
from contextlib import ExitStack

import concourse.bass as bass
import concourse.mybir as mybir
from concourse.bass_utils import run_bass_kernel_spmd

F32 = mybir.dt.float32
BF16 = mybir.dt.bfloat16
AF = mybir.ActivationFunctionType
ALU = mybir.AluOpType
AX = mybir.AxisListType

NCORE = 8
TPC = 2048
W = 512



class KB:

    ENG = ("sync", "vector", "scalar", "tensor", "gpsimd")

    def __init__(self, nc, ctx):
        self.nc = nc
        self.items = {e: [] for e in self.ENG}
        self.cum = dict.fromkeys(self.ENG, 0)
        self.know = {e: {} for e in self.ENG}
        self.svals = {e: [] for e in self.ENG}
        self.snaps = {e: [] for e in self.ENG}
        self.last_w = {}
        self.readers = {}
        self.sems = {e: ctx.enter_context(nc.semaphore("sem_" + e)) for e in self.ENG}

    def _snap_at(self, e, v):
        i = bisect.bisect_right(self.svals[e], v) - 1
        return self.snaps[e][i] if i >= 0 else {}

    def op(self, eng, fn, reads=(), writes=(), dma=False):
        deps = []
        for k in reads:
            t = self.last_w.get(k)
            if t is not None:
                deps.append(t)
        for k in writes:
            t = self.last_w.get(k)
            if t is not None:
                deps.append(t)
            deps.extend(self.readers.get(k, {}).values())
        kn = self.know[eng]
        need = {}
        for (de, dv) in deps:
            if de == eng and eng == "tensor":
                continue
            if kn.get(de, -1) >= dv:
                continue
            if need.get(de, -1) < dv:
                need[de] = dv
        for de, dv in need.items():
            if kn.get(de, -1) < dv:
                kn[de] = dv
            for s2, v2 in self._snap_at(de, dv).items():
                if kn.get(s2, -1) < v2:
                    kn[s2] = v2
        step = 16 if dma else 1
        self.cum[eng] += step
        tok = (eng, self.cum[eng])
        self.items[eng].append((fn, sorted(need.items()), step))
        self.svals[eng].append(tok[1])
        sn = dict(kn)
        sn[eng] = tok[1]
        self.snaps[eng].append(sn)
        for k in reads:
            self.readers.setdefault(k, {})[eng] = tok
        for k in writes:
            self.last_w[k] = tok
            self.readers[k] = {}
        return tok

    def dma(self, q, out, in_, reads=(), writes=()):
        return self.op(q, lambda e, o=out, i=in_: e.dma_start(out=o, in_=i),
                       reads, writes, dma=True)

    def mm(self, out, lhsT, rhs, start, stop, reads=(), writes=()):
        return self.op("tensor",
                       lambda e, o=out, l=lhsT, r=rhs, s1=start, s2=stop:
                       e.matmul(o, l, r, start=s1, stop=s2), reads, writes)

    def tr(self, out, in_, ident, reads=(), writes=()):
        return self.op("tensor",
                       lambda e, o=out, i=in_, d=ident: e.transpose(o, i, d),
                       reads, writes)

    def act(self, out, in_, func, bias=0.0, scale=1.0, accum=None, reads=(), writes=()):
        def f(e, o=out, i=in_, fu=func, b=bias, s=scale, a=accum):
            if a is None:
                return e.activation(out=o, in_=i, func=fu, bias=b, scale=s)
            return e.activation(out=o, in_=i, func=fu, bias=b, scale=s, accum_out=a)
        return self.op("scalar", f, reads, writes)

    def v(self, method, *args, reads=(), writes=(), **kw):
        return self.op("vector",
                       lambda e, m=method, a=args, k=kw: getattr(e, m)(*a, **k),
                       reads, writes)

    def emit(self):
        with self.nc.Block() as blk:
            for eng in self.ENG:
                if not self.items[eng]:
                    continue

                def make(eng=eng):
                    def body(e):
                        for fn, waits, step in self.items[eng]:
                            for weng, wval in waits:
                                e.wait_ge(self.sems[weng], wval)
                            inst = fn(e)
                            inst.then_inc(self.sems[eng], step)
                        e.wait_ge(self.sems[eng], self.cum[eng])
                    return body
                getattr(blk, eng)(make())


def emit_ln(kb, tag, nch, src, dst, aux, dim, eps=1e-5):
    sq, ksq = aux["sq"]
    rows, krows = aux["rows"]
    psS, kS = aux["psS"]
    psQ, kQ = aux["psQ"]
    psA, kA = aux["psA"]
    psB, kB = aux["psB"]
    onesK, kone = aux["onesK"]
    ones1, kone1 = aux["ones1"]
    for c in range(nch):
        sap, skey = src(c)
        kb.act(sq, sap, AF.Square, reads=[skey], writes=[ksq])
        kb.mm(psS, onesK, sap, c == 0, c == nch - 1, reads=[skey, kone], writes=[kS])
        kb.mm(psQ, onesK, sq, c == 0, c == nch - 1, reads=[ksq, kone], writes=[kQ])
    r = lambda i: rows[0:1, i * W:(i + 1) * W]
    kb.v("tensor_scalar", r(0), psS, 1.0 / dim, None, ALU.mult, reads=[kS], writes=[krows])
    kb.v("tensor_scalar", r(1), psQ, 1.0 / dim, None, ALU.mult, reads=[kQ, krows], writes=[krows])
    kb.v("tensor_tensor", r(2), r(0), r(0), ALU.mult, reads=[krows], writes=[krows])
    kb.v("tensor_tensor", r(1), r(1), r(2), ALU.subtract, reads=[krows], writes=[krows])
    kb.v("tensor_scalar", r(1), r(1), eps, None, ALU.add, reads=[krows], writes=[krows])
    kb.act(r(2), r(1), AF.Sqrt, reads=[krows], writes=[krows])
    kb.v("reciprocal", r(3), r(2), reads=[krows], writes=[krows])
    kb.v("tensor_tensor", r(0), r(0), r(3), ALU.mult, reads=[krows], writes=[krows])
    kb.v("tensor_scalar", r(0), r(0), -1.0, None, ALU.mult, reads=[krows], writes=[krows])
    kb.mm(psA, ones1, r(3), True, True, reads=[krows, kone1], writes=[kA])
    kb.mm(psB, ones1, r(0), True, True, reads=[krows, kone1], writes=[kB])
    for c in range(nch):
        sap, skey = src(c)
        dap, dkey = dst(c)
        kb.v("tensor_tensor", sq, sap, psA, ALU.mult, reads=[skey, kA], writes=[ksq])
        kb.v("tensor_tensor", dap, sq, psB, ALU.add, reads=[ksq, kB], writes=[dkey])


def _mk(nc, ctx, name, shape, dt):
    return ctx.enter_context(nc.sbuf_tensor(name, shape, dt))


def _mkp(nc, ctx, name, shape, dt=F32):
    return ctx.enter_context(nc.psum_tensor(name, shape, dt))


def _ln_aux(kb, nc, ctx, ident=False):
    aux = {}
    sq = _mk(nc, ctx, "ln_sq", [128, W], F32)
    rows = _mk(nc, ctx, "ln_rows", [1, 4 * W], F32)
    onesK = _mk(nc, ctx, "onesK", [128, 1], F32)
    ones1 = _mk(nc, ctx, "ones1", [1, 128], F32)
    psS = _mkp(nc, ctx, "psS", [1, W])
    psQ = _mkp(nc, ctx, "psQ", [1, W])
    psA = _mkp(nc, ctx, "psA", [128, W])
    psB = _mkp(nc, ctx, "psB", [128, W])
    kb.v("memset", onesK[:], 1.0, writes=["onesK"])
    kb.v("memset", ones1[:], 1.0, writes=["ones1"])
    aux["sq"] = (sq[:], "ln_sq")
    aux["rows"] = (rows, "ln_rows")
    aux["psS"] = (psS[:], "psS")
    aux["psQ"] = (psQ[:], "psQ")
    aux["psA"] = (psA[:], "psA")
    aux["psB"] = (psB[:], "psB")
    aux["onesK"] = (onesK[:], "onesK")
    aux["ones1"] = (ones1[:], "ones1")
    return aux, psA, psB, onesK, ones1



def build_ffn0():
    nc = bass.Bass("TRN2", target_bir_lowering=False)
    XE = nc.dram_tensor("XE", [1024, 2560], F32, kind="ExternalInput").ap()
    WRT = nc.dram_tensor("WRT", [1024, 1024], F32, kind="ExternalInput").ap()
    WKT = nc.dram_tensor("WKT", [1024, 1024], F32, kind="ExternalInput").ap()
    WVT = nc.dram_tensor("WVT", [1024, 1024], F32, kind="ExternalInput").ap()
    MUR = nc.dram_tensor("MUR", [1024, 1], F32, kind="ExternalInput").ap()
    MUK = nc.dram_tensor("MUK", [1024, 1], F32, kind="ExternalInput").ap()
    XO = nc.dram_tensor("XO", [1024, 2048], F32, kind="ExternalOutput").ap()
    with ExitStack() as ctx:
        kb = KB(nc, ctx)
        w = _mk(nc, ctx, "w", [128, 3 * 8192], F32)
        xb = _mk(nc, ctx, "xb", [128, 8 * W], F32)
        lnb = _mk(nc, ctx, "lnb", [128, 8 * W], F32)
        lnp = _mk(nc, ctx, "lnp", [128, 8], F32)
        xr = _mk(nc, ctx, "xr", [128, 8 * W], F32)
        xk = _mk(nc, ctx, "xk", [128, 8 * W], F32)
        rr = _mk(nc, ctx, "rr", [128, 8 * W], F32)
        mur = _mk(nc, ctx, "mur", [128, 8], F32)
        mur1 = _mk(nc, ctx, "mur1", [128, 8], F32)
        muk = _mk(nc, ctx, "muk", [128, 8], F32)
        muk1 = _mk(nc, ctx, "muk1", [128, 8], F32)
        aux, psA, psB, onesK, ones1 = _ln_aux(kb, nc, ctx)
        psM0 = _mkp(nc, ctx, "psM0", [128, W])
        psM1 = _mkp(nc, ctx, "psM1", [128, W])
        for wi, wd in enumerate((WRT, WKT, WVT)):
            for kc in range(8):
                kb.dma("sync", w[:, wi * 8192 + kc * 1024: wi * 8192 + (kc + 1) * 1024],
                       wd[kc * 128:(kc + 1) * 128, :], writes=["w"])
        for kc in range(8):
            kb.dma("sync", mur[:, kc:kc + 1], MUR[kc * 128:(kc + 1) * 128, :], writes=["mur"])
            kb.dma("sync", muk[:, kc:kc + 1], MUK[kc * 128:(kc + 1) * 128, :], writes=["muk"])
        kb.v("tensor_scalar", mur1[:], mur[:], -1.0, 1.0, ALU.mult, ALU.add,
             reads=["mur"], writes=["mur1"])
        kb.v("tensor_scalar", muk1[:], muk[:], -1.0, 1.0, ALU.mult, ALU.add,
             reads=["muk"], writes=["muk1"])

        for b in range(-1, 4):
            co = 512 * (b + 1)
            for kc in range(8):
                kb.dma("sync", xb[:, kc * W:(kc + 1) * W],
                       XE[kc * 128:(kc + 1) * 128, co:co + W], writes=["xb"])
            emit_ln(kb, "l", 8,
                    lambda c: (xb[:, c * W:(c + 1) * W], "xb"),
                    lambda c: (lnb[:, c * W:(c + 1) * W], "lnb"), aux, 1024)
            if b >= 0:
                for kc in range(8):
                    for dst, mu, mu1, dk, mk in ((xr, mur, mur1, "xr", "mur"),
                                                 (xk, muk, muk1, "xk", "muk")):
                        d0 = dst[:, kc * W: kc * W + 1]
                        kb.v("tensor_scalar", d0, lnp[:, kc:kc + 1], mu1[:, kc:kc + 1],
                             None, ALU.mult, reads=["lnp", mk + "1"], writes=[dk])
                        kb.v("scalar_tensor_tensor", d0, lnb[:, kc * W: kc * W + 1],
                             mu[:, kc:kc + 1], d0, ALU.mult, ALU.add,
                             reads=["lnb", mk, dk], writes=[dk])
                        dmain = dst[:, kc * W + 1: kc * W + W]
                        kb.v("tensor_scalar", dmain, lnb[:, kc * W: kc * W + W - 1],
                             mu1[:, kc:kc + 1], None, ALU.mult,
                             reads=["lnb", mk + "1"], writes=[dk])
                        kb.v("scalar_tensor_tensor", dmain, lnb[:, kc * W + 1: kc * W + W],
                             mu[:, kc:kc + 1], dmain, ALU.mult, ALU.add,
                             reads=["lnb", mk, dk], writes=[dk])
            for kc in range(8):
                kb.v("tensor_copy", lnp[:, kc:kc + 1], lnb[:, kc * W + W - 1: kc * W + W],
                     reads=["lnb"], writes=["lnp"])
            if b < 0:
                continue
            for dc in range(8):
                ps = (psM0, "psM0") if dc % 2 == 0 else (psM1, "psM1")
                for kc in range(8):
                    kb.mm(ps[0][:], w[:, 0 * 8192 + kc * 1024 + dc * 128: 0 * 8192 + kc * 1024 + dc * 128 + 128],
                          xr[:, kc * W:(kc + 1) * W], kc == 0, kc == 7,
                          reads=["w", "xr"], writes=[ps[1]])
                kb.act(rr[:, dc * W:(dc + 1) * W], ps[0][:], AF.Sigmoid,
                       reads=[ps[1]], writes=["rr"])
            for dc in range(8):
                ps = (psM0, "psM0") if dc % 2 == 0 else (psM1, "psM1")
                for kc in range(8):
                    kb.mm(ps[0][:], w[:, 1 * 8192 + kc * 1024 + dc * 128: 1 * 8192 + kc * 1024 + dc * 128 + 128],
                          xk[:, kc * W:(kc + 1) * W], kc == 0, kc == 7,
                          reads=["w", "xk"], writes=[ps[1]])
                sl = xr[:, dc * W:(dc + 1) * W]
                kb.act(sl, ps[0][:], AF.Relu, reads=[ps[1]], writes=["xr"])
                kb.v("tensor_tensor", sl, sl, sl, ALU.mult, reads=["xr"], writes=["xr"])
            for dc in range(8):
                ps = (psM0, "psM0") if dc % 2 == 0 else (psM1, "psM1")
                for kc in range(8):
                    kb.mm(ps[0][:], w[:, 2 * 8192 + kc * 1024 + dc * 128: 2 * 8192 + kc * 1024 + dc * 128 + 128],
                          xr[:, kc * W:(kc + 1) * W], kc == 0, kc == 7,
                          reads=["w", "xr"], writes=[ps[1]])
                sl = xk[:, dc * W:(dc + 1) * W]
                kb.v("tensor_tensor", sl, ps[0][:], rr[:, dc * W:(dc + 1) * W],
                     ALU.mult, reads=[ps[1], "rr"], writes=["xk"])
                ol = lnb[:, dc * W:(dc + 1) * W]
                kb.v("scalar_tensor_tensor", ol, sl, 0.5, xb[:, dc * W:(dc + 1) * W],
                     ALU.mult, ALU.add, reads=["xk", "xb"], writes=["lnb"])
                kb.dma("gpsimd", XO[dc * 128:(dc + 1) * 128, b * W:(b + 1) * W], ol,
                       reads=["lnb"])
        kb.emit()
    return nc



def build_attn():
    nc = bass.Bass("TRN2", target_bir_lowering=False)
    XB = nc.dram_tensor("XB", [1024, 4096], F32, kind="ExternalInput").ap()
    WSL = nc.dram_tensor("WSL", [1024, 1536], F32, kind="ExternalInput").ap()
    BSL = nc.dram_tensor("BSL", [1536, 1], F32, kind="ExternalInput").ap()
    AQ = nc.dram_tensor("AQ", [1024, 8], F32, kind="ExternalInput").ap()
    BQSL = nc.dram_tensor("BQSL", [8, 1536], F32, kind="ExternalInput").ap()
    IDB = nc.dram_tensor("IDB", [128, 128], BF16, kind="ExternalInput").ap()
    AO = nc.dram_tensor("AO", [512, 4096], F32, kind="ExternalOutput").ap()
    with ExitStack() as ctx:
        kb = KB(nc, ctx)
        stage = _mk(nc, ctx, "stage", [128, W], F32)
        wsl = _mk(nc, ctx, "wsl", [128, 8 * 1536], BF16)
        aqb = _mk(nc, ctx, "aqb", [128, 64], BF16)
        bqb = _mk(nc, ctx, "bqb", [8, 1536], BF16)
        bqs = _mk(nc, ctx, "bqs", [8, 1536], F32)
        bsl = _mk(nc, ctx, "bsl", [128, 12], F32)
        xb = _mk(nc, ctx, "xb", [128, 8 * W], F32)
        lnb = _mk(nc, ctx, "lnb", [128, 8 * W], BF16)
        qkv = _mk(nc, ctx, "qkv", [128, 12 * 4096], BF16)
        vaug = _mk(nc, ctx, "vaug", [128, 8 * 2080], BF16)
        expst = _mk(nc, ctx, "expst", [128, 2 * W], BF16)
        ubf = _mk(nc, ctx, "ubf", [8, W], BF16)
        idb = _mk(nc, ctx, "idb", [128, 128], BF16)
        rrow = _mk(nc, ctx, "rrow", [1, W], F32)
        bcs = _mk(nc, ctx, "bcs", [64, W], F32)
        oev = _mk(nc, ctx, "oev", [64, W], F32)
        aux, psA, psB, onesK, ones1 = _ln_aux(kb, nc, ctx)
        psQK0 = _mkp(nc, ctx, "psQK0", [128, W])
        psQK1 = _mkp(nc, ctx, "psQK1", [128, W])
        psU = _mkp(nc, ctx, "psU", [8, W])
        psTR = _mkp(nc, ctx, "psTR", [128, 64], BF16)
        kb.dma("sync", idb[:], IDB, writes=["idb"])
        for kc in range(8):
            for j in range(3):
                kb.dma("sync", stage[:], WSL[kc * 128:(kc + 1) * 128, j * 512:(j + 1) * 512],
                       writes=["stage"])
                kb.v("tensor_copy", wsl[:, kc * 1536 + j * 512: kc * 1536 + (j + 1) * 512],
                     stage[:], reads=["stage"], writes=["wsl"])
        for kc in range(8):
            kb.dma("sync", stage[:, 0:8], AQ[kc * 128:(kc + 1) * 128, :], writes=["stage"])
            kb.v("tensor_copy", aqb[:, kc * 8:(kc + 1) * 8], stage[:, 0:8],
                 reads=["stage"], writes=["aqb"])
        kb.dma("sync", bqs[:], BQSL, writes=["bqs"])
        kb.v("tensor_copy", bqb[:], bqs[:], reads=["bqs"], writes=["bqb"])
        for oc in range(12):
            kb.dma("sync", bsl[:, oc:oc + 1], BSL[oc * 128:(oc + 1) * 128, :], writes=["bsl"])
        for blk in range(8):
            for kc in range(8):
                kb.dma("sync", xb[:, kc * W:(kc + 1) * W],
                       XB[kc * 128:(kc + 1) * 128, blk * W:(blk + 1) * W], writes=["xb"])
            emit_ln(kb, "l", 8,
                    lambda c: (xb[:, c * W:(c + 1) * W], "xb"),
                    lambda c: (lnb[:, c * W:(c + 1) * W], "lnb"), aux, 1024)
            for kc in range(8):
                kb.mm(psU[:], aqb[:, kc * 8:(kc + 1) * 8], lnb[:, kc * W:(kc + 1) * W],
                      kc == 0, kc == 7, reads=["aqb", "lnb"], writes=["psU"])
            kb.act(ubf[:], psU[:], AF.Copy, scale=2.0, reads=["psU"], writes=["ubf"])
            for oc in range(12):
                ps = (psQK0, "psQK0") if oc % 2 == 0 else (psQK1, "psQK1")
                for kc in range(8):
                    kb.mm(ps[0][:], wsl[:, kc * 1536 + oc * 128: kc * 1536 + oc * 128 + 128],
                          lnb[:, kc * W:(kc + 1) * W], kc == 0, False,
                          reads=["wsl", "lnb"], writes=[ps[1]])
                kb.mm(ps[0][:], bqb[:, oc * 128:(oc + 1) * 128], ubf[:], False, True,
                      reads=["bqb", "ubf"], writes=[ps[1]])
                kb.act(qkv[:, oc * 4096 + blk * W: oc * 4096 + (blk + 1) * W], ps[0][:],
                       AF.Identity, bias=bsl[:, oc:oc + 1], reads=[ps[1], "bsl"], writes=["qkv"])
        for h in range(8):
            base = (h % 2) * 64
            voc = 8 + h // 2
            for i in range(32):
                kb.tr(psTR[:], qkv[base:base + 64, voc * 4096 + i * 128: voc * 4096 + (i + 1) * 128],
                      idb[base:base + 64, base:base + 64], reads=["qkv", "idb"], writes=["psTR"])
                kb.act(vaug[:, h * 2080 + i * 65: h * 2080 + i * 65 + 64], psTR[:],
                       AF.Copy, reads=["psTR"], writes=["vaug"])
            kb.v("memset", vaug[:, h * 2080 + 64: h * 2080 + 2080: 65], 1.0, writes=["vaug"])
        for h in range(8):
            base = (h % 2) * 64
            qoc, koc = h // 2, 4 + h // 2
            for qb in range(8):
                qrhs = qkv[base:base + 64, qoc * 4096 + qb * W: qoc * 4096 + (qb + 1) * W]
                for i in range(32):
                    ps = (psA, "psA") if i % 2 == 0 else (psB, "psB")
                    kb.mm(ps[0][:], qkv[base:base + 64, koc * 4096 + i * 128: koc * 4096 + i * 128 + 128],
                          qrhs, True, True, reads=["qkv"], writes=[ps[1]])
                    eap = expst[:, (i % 2) * W:(i % 2 + 1) * W]
                    ek = "exp%d" % (i % 2)
                    kb.act(eap, ps[0][:], AF.Exp, scale=0.125, reads=[ps[1]], writes=[ek])
                    kb.mm(psQK0[0:65, :], vaug[:, h * 2080 + i * 65: h * 2080 + (i + 1) * 65],
                          eap, i == 0, i == 31, reads=["vaug", ek], writes=["psQK0"])
                kb.v("reciprocal", rrow[:], psQK0[64:65, :], reads=["psQK0"], writes=["rrow"])
                kb.mm(psQK1[0:64, :], ones1[0:1, 0:64], rrow[:], True, True,
                      reads=["ones1", "rrow"], writes=["psQK1"])
                kb.act(bcs[:], psQK1[0:64, :], AF.Copy, reads=["psQK1"], writes=["bcs"])
                kb.v("tensor_tensor", oev[:], psQK0[0:64, :], bcs[:], ALU.mult,
                     reads=["psQK0", "bcs"], writes=["oev"])
                kb.dma("gpsimd", AO[h * 64:(h + 1) * 64, qb * W:(qb + 1) * W], oev[:],
                       reads=["oev"])
        kb.emit()
    return nc



def build_projres():
    nc = bass.Bass("TRN2", target_bir_lowering=False)
    AT = nc.dram_tensor("AT", [1024, 2048], F32, kind="ExternalInput").ap()
    XR = nc.dram_tensor("XR", [1024, 2048], F32, kind="ExternalInput").ap()
    WP = nc.dram_tensor("WP", [1024, 1024], F32, kind="ExternalInput").ap()
    BP = nc.dram_tensor("BP", [1024, 1], F32, kind="ExternalInput").ap()
    APJ = nc.dram_tensor("APJ", [1024, 8], F32, kind="ExternalInput").ap()
    BPJ = nc.dram_tensor("BPJ", [8, 1024], F32, kind="ExternalInput").ap()
    XO = nc.dram_tensor("XO", [1024, 2048], F32, kind="ExternalOutput").ap()
    with ExitStack() as ctx:
        kb = KB(nc, ctx)
        w = _mk(nc, ctx, "w", [128, 8192], F32)
        at = _mk(nc, ctx, "at", [128, 8 * W], F32)
        xr = _mk(nc, ctx, "xr", [128, 8 * W], F32)
        ob = _mk(nc, ctx, "ob", [128, 8 * W], F32)
        apj = _mk(nc, ctx, "apj", [128, 64], F32)
        bpj = _mk(nc, ctx, "bpj", [8, 1024], F32)
        bp = _mk(nc, ctx, "bp", [128, 8], F32)
        uu = _mk(nc, ctx, "uu", [8, W], F32)
        psM0 = _mkp(nc, ctx, "psM0", [128, W])
        psM1 = _mkp(nc, ctx, "psM1", [128, W])
        psU = _mkp(nc, ctx, "psU", [8, W])
        for kc in range(8):
            kb.dma("sync", w[:, kc * 1024:(kc + 1) * 1024], WP[kc * 128:(kc + 1) * 128, :],
                   writes=["w"])
            kb.dma("sync", apj[:, kc * 8:(kc + 1) * 8], APJ[kc * 128:(kc + 1) * 128, :],
                   writes=["apj"])
            kb.dma("sync", bp[:, kc:kc + 1], BP[kc * 128:(kc + 1) * 128, :], writes=["bp"])
        kb.dma("sync", bpj[:], BPJ, writes=["bpj"])
        for blk in range(4):
            for kc in range(8):
                kb.dma("sync", at[:, kc * W:(kc + 1) * W],
                       AT[kc * 128:(kc + 1) * 128, blk * W:(blk + 1) * W], writes=["at"])
                kb.dma("sync", xr[:, kc * W:(kc + 1) * W],
                       XR[kc * 128:(kc + 1) * 128, blk * W:(blk + 1) * W], writes=["xr"])
            for kc in range(8):
                kb.mm(psU[:], apj[:, kc * 8:(kc + 1) * 8], at[:, kc * W:(kc + 1) * W],
                      kc == 0, kc == 7, reads=["apj", "at"], writes=["psU"])
            kb.act(uu[:], psU[:], AF.Copy, scale=2.0, reads=["psU"], writes=["uu"])
            for dc in range(8):
                ps = (psM0, "psM0") if dc % 2 == 0 else (psM1, "psM1")
                for kc in range(8):
                    kb.mm(ps[0][:], w[:, kc * 1024 + dc * 128: kc * 1024 + dc * 128 + 128],
                          at[:, kc * W:(kc + 1) * W], kc == 0, False,
                          reads=["w", "at"], writes=[ps[1]])
                kb.mm(ps[0][:], bpj[:, dc * 128:(dc + 1) * 128], uu[:], False, True,
                      reads=["bpj", "uu"], writes=[ps[1]])
                sl = ob[:, dc * W:(dc + 1) * W]
                kb.act(sl, ps[0][:], AF.Identity, bias=bp[:, dc:dc + 1],
                       reads=[ps[1], "bp"], writes=["ob"])
                kb.v("tensor_tensor", sl, sl, xr[:, dc * W:(dc + 1) * W], ALU.add,
                     reads=["ob", "xr"], writes=["ob"])
                kb.dma("gpsimd", XO[dc * 128:(dc + 1) * 128, blk * W:(blk + 1) * W], sl,
                       reads=["ob"])
        kb.emit()
    return nc



def build_moe():
    nc = bass.Bass("TRN2", target_bir_lowering=False)
    XM = nc.dram_tensor("XM", [1024, 2048], F32, kind="ExternalInput").ap()
    GWT = nc.dram_tensor("GWT", [1024, 16], F32, kind="ExternalInput").ap()
    BIT = nc.dram_tensor("BIT", [128, 16], F32, kind="ExternalInput").ap()
    LTM = nc.dram_tensor("LTM", [128, 256], F32, kind="ExternalInput").ap()
    EUP = nc.dram_tensor("EUP", [20, 1024, 512], F32, kind="ExternalInput").ap()
    EDN = nc.dram_tensor("EDN", [20, 512, 1024], F32, kind="ExternalInput").ap()
    IDF = nc.dram_tensor("IDF", [128, 128], F32, kind="ExternalInput").ap()
    XO = nc.dram_tensor("XO", [1024, 2048], F32, kind="ExternalOutput").ap()
    with ExitStack() as ctx:
        kb = KB(nc, ctx)
        lnbf = _mk(nc, ctx, "lnbf", [128, 8 * 2048], BF16)
        acc = _mk(nc, ctx, "acc", [128, 8 * 2048], F32)
        wst = _mk(nc, ctx, "wst", [128, 4096], F32)
        ubf = _mk(nc, ctx, "ubf", [128, 4096], BF16)
        dbf = _mk(nc, ctx, "dbf", [128, 4096], BF16)
        hbf = _mk(nc, ctx, "hbf", [128, 4 * 2048], BF16)
        gbf = _mk(nc, ctx, "gbf", [128, 8 * 16], BF16)
        bit = _mk(nc, ctx, "bit", [128, 16], F32)
        ltm = _mk(nc, ctx, "ltm", [128, 256], F32)
        idf = _mk(nc, ctx, "idf", [128, 128], F32)
        lgf = _mk(nc, ctx, "lgf", [16, W], F32)
        sig = _mk(nc, ctx, "sig", [128, 16], F32)
        sc = _mk(nc, ctx, "sc", [128, 16], F32)
        gs = _mk(nc, ctx, "gs", [128, 4], F32)
        gp = _mk(nc, ctx, "gp", [128, 4], F32)
        gcnt = _mk(nc, ctx, "gcnt", [128, 4], F32)
        msk = _mk(nc, ctx, "msk", [128, 16], F32)
        e256 = _mk(nc, ctx, "e256", [128, 256], F32)
        ecnt = _mk(nc, ctx, "ecnt", [128, 16], F32)
        e256b = _mk(nc, ctx, "e256b", [128, 256], F32)
        ecnt2 = _mk(nc, ctx, "ecnt2", [128, 16], F32)
        dcol = _mk(nc, ctx, "dcol", [128, 1], F32)
        rcol = _mk(nc, ctx, "rcol", [128, 1], F32)
        cmt = _mk(nc, ctx, "cmt", [128, 16], F32)
        cfm = _mk(nc, ctx, "cfm", [16, 2048], F32)
        crow = _mk(nc, ctx, "crow", [1, 2048], F32)
        cbs = _mk(nc, ctx, "cbs", [128, W], F32)
        rlu = _mk(nc, ctx, "rlu", [128, W], F32)
        sqt = _mk(nc, ctx, "sqt", [128, W], F32)
        xs = _mk(nc, ctx, "xs", [128, W], F32)
        osb = _mk(nc, ctx, "osb", [128, W], F32)
        xbl = _mk(nc, ctx, "xbl", [128, 8 * W], F32)
        aux, psA, psB, onesK, ones1 = _ln_aux(kb, nc, ctx)
        psC = _mkp(nc, ctx, "psC", [128, W])
        psE = _mkp(nc, ctx, "psE", [128, W])
        kb.dma("sync", bit[:], BIT, writes=["bit"])
        kb.dma("sync", ltm[:], LTM, writes=["ltm"])
        kb.dma("sync", idf[:], IDF, writes=["idf"])
        for kc in range(8):
            kb.dma("sync", wst[:, 0:16], GWT[kc * 128:(kc + 1) * 128, :], writes=["wst"])
            kb.v("tensor_copy", gbf[:, kc * 16:(kc + 1) * 16], wst[:, 0:16],
                 reads=["wst"], writes=["gbf"])
        for blk in range(4):
            for kc in range(8):
                kb.dma("sync", xbl[:, kc * W:(kc + 1) * W],
                       XM[kc * 128:(kc + 1) * 128, blk * W:(blk + 1) * W], writes=["xbl"])
            emit_ln(kb, "l", 8,
                    lambda c: (xbl[:, c * W:(c + 1) * W], "xbl"),
                    lambda c, blk=blk: (lnbf[:, c * 2048 + blk * W: c * 2048 + (blk + 1) * W], "lnbf"),
                    aux, 1024)
        for blk in range(4):
            for kc in range(8):
                kb.mm(psA[0:16, :], gbf[:, kc * 16:(kc + 1) * 16],
                      lnbf[:, kc * 2048 + blk * W: kc * 2048 + (blk + 1) * W],
                      kc == 0, kc == 7, reads=["gbf", "lnbf"], writes=["psA"])
            kb.act(lgf[:], psA[0:16, :], AF.Copy, reads=["psA"], writes=["lgf"])
            for i in range(4):
                t = blk * 4 + i
                kb.tr(psB[0:128, 0:16], lgf[:, i * 128:(i + 1) * 128], idf[0:16, 0:16],
                      reads=["lgf", "idf"], writes=["psB"])
                kb.act(sig[:], psB[0:128, 0:16], AF.Sigmoid, reads=["psB"], writes=["sig"])
                kb.v("tensor_tensor", sc[:], sig[:], bit[:], ALU.add,
                     reads=["sig", "bit"], writes=["sc"])
                pairs = [(0, 1), (0, 2), (0, 3), (1, 2), (1, 3), (2, 3)]
                scg = sc[:].rearrange("p (g q) -> p g q", q=4)
                for pi, (i1, i2) in enumerate(pairs):
                    kb.v("tensor_tensor", gp[:], scg[:, :, i1], scg[:, :, i2],
                         ALU.add, reads=["sc"], writes=["gp"])
                    if pi == 0:
                        kb.v("tensor_copy", gs[:], gp[:], reads=["gp"], writes=["gs"])
                    else:
                        kb.v("tensor_tensor", gs[:], gs[:], gp[:], ALU.max,
                             reads=["gs", "gp"], writes=["gs"])
                kb.v("tensor_tensor", e256[:, 0:16].rearrange("p (g q) -> p g q", g=4),
                     gs[:].unsqueeze(1).broadcast_to([128, 4, 4]),
                     gs[:].unsqueeze(2).broadcast_to([128, 4, 4]), ALU.is_gt,
                     reads=["gs"], writes=["e256"])
                kb.v("reduce_sum", gcnt[:], e256[:, 0:16].rearrange("p (g q) -> p g q", g=4),
                     AX.X, reads=["e256"], writes=["gcnt"])
                kb.v("tensor_scalar", gcnt[:], gcnt[:], 1.5, None, ALU.is_lt,
                     reads=["gcnt"], writes=["gcnt"])
                kb.v("tensor_tensor", msk[:].rearrange("p (g q) -> p g q", g=4),
                     sc[:].rearrange("p (g q) -> p g q", g=4),
                     gcnt[:].unsqueeze(2).broadcast_to([128, 4, 4]), ALU.mult,
                     reads=["sc", "gcnt"], writes=["msk"])
                kb.v("tensor_tensor", e256[:].rearrange("p (a b) -> p a b", a=16),
                     msk[:].unsqueeze(1).broadcast_to([128, 16, 16]),
                     msk[:].unsqueeze(2).broadcast_to([128, 16, 16]), ALU.is_gt,
                     reads=["msk"], writes=["e256"])
                kb.v("reduce_sum", ecnt[:], e256[:].rearrange("p (a b) -> p a b", a=16),
                     AX.X, reads=["e256"], writes=["ecnt"])
                kb.v("tensor_tensor", e256b[:].rearrange("p (a b) -> p a b", a=16),
                     msk[:].unsqueeze(1).broadcast_to([128, 16, 16]),
                     msk[:].unsqueeze(2).broadcast_to([128, 16, 16]), ALU.is_equal,
                     reads=["msk"], writes=["e256b"])
                kb.v("tensor_tensor", e256b[:], e256b[:], ltm[:], ALU.mult,
                     reads=["e256b", "ltm"], writes=["e256b"])
                kb.v("reduce_sum", ecnt2[:], e256b[:].rearrange("p (a b) -> p a b", a=16),
                     AX.X, reads=["e256b"], writes=["ecnt2"])
                kb.v("tensor_tensor", ecnt[:], ecnt[:], ecnt2[:], ALU.add,
                     reads=["ecnt", "ecnt2"], writes=["ecnt"])
                kb.v("tensor_scalar", ecnt[:], ecnt[:], 3.5, None, ALU.is_lt,
                     reads=["ecnt"], writes=["ecnt"])
                kb.v("tensor_tensor", cmt[:], sig[:], ecnt[:], ALU.mult,
                     reads=["sig", "ecnt"], writes=["cmt"])
                kb.v("reduce_sum", dcol[:], cmt[:], AX.X, reads=["cmt"], writes=["dcol"])
                kb.v("tensor_scalar", dcol[:], dcol[:], 1e-20, None, ALU.add,
                     reads=["dcol"], writes=["dcol"])
                kb.v("reciprocal", rcol[:], dcol[:], reads=["dcol"], writes=["rcol"])
                kb.v("tensor_scalar", cmt[:], cmt[:], rcol[:], 2.5, ALU.mult, ALU.mult,
                     reads=["cmt", "rcol"], writes=["cmt"])
                kb.tr(psB[0:16, 128:256], cmt[:, 0:16], idf[:],
                      reads=["cmt", "idf"], writes=["psB"])
                kb.act(cfm[:, t * 128:(t + 1) * 128], psB[0:16, 128:256], AF.Copy,
                       reads=["psB"], writes=["cfm"])
        for e in range(20):
            for kc in range(8):
                kb.dma("sync", wst[:, kc * 512:(kc + 1) * 512],
                       EUP[e, kc * 128:(kc + 1) * 128, :], writes=["wst"])
            for kc in range(8):
                kb.v("tensor_copy", ubf[:, kc * 512:(kc + 1) * 512],
                     wst[:, kc * 512:(kc + 1) * 512], reads=["wst"], writes=["ubf"])
            if e < 16:
                kb.dma("gpsimd", crow[0:1, :], cfm[e:e + 1, :], reads=["cfm"], writes=["crow"])
            for blk in range(4):
                if e < 16:
                    kb.mm(psC[:], ones1[:], crow[0:1, blk * W:(blk + 1) * W], True, True,
                          reads=["ones1", "crow"], writes=["psC"])
                    kb.act(cbs[:], psC[:], AF.Copy, reads=["psC"], writes=["cbs"])
                for ic in range(4):
                    ps = (psA, "psA") if ic % 2 == 0 else (psB, "psB")
                    for kc in range(8):
                        kb.mm(ps[0][:], ubf[:, kc * 512 + ic * 128: kc * 512 + ic * 128 + 128],
                              lnbf[:, kc * 2048 + blk * W: kc * 2048 + (blk + 1) * W],
                              kc == 0, kc == 7, reads=["ubf", "lnbf"], writes=[ps[1]])
                    kb.v("tensor_scalar", rlu[:], ps[0][:], 0.0, None, ALU.max,
                         reads=[ps[1]], writes=["rlu"])
                    kb.act(sqt[:], rlu[:], AF.Square, reads=["rlu"], writes=["sqt"])
                    hsl = hbf[:, ic * 2048 + blk * W: ic * 2048 + (blk + 1) * W]
                    if e < 16:
                        kb.v("tensor_tensor", hsl, sqt[:], cbs[:], ALU.mult,
                             reads=["sqt", "cbs"], writes=["hbf"])
                    else:
                        kb.v("tensor_copy", hsl, sqt[:], reads=["sqt"], writes=["hbf"])
            for ic in range(4):
                kb.dma("sync", wst[:, ic * 1024:(ic + 1) * 1024],
                       EDN[e, ic * 128:(ic + 1) * 128, :], writes=["wst"])
            for ic in range(4):
                kb.v("tensor_copy", dbf[:, ic * 1024:(ic + 1) * 1024],
                     wst[:, ic * 1024:(ic + 1) * 1024], reads=["wst"], writes=["dbf"])
            for blk in range(4):
                for dc in range(8):
                    ps = (psC, "psC") if dc % 2 == 0 else (psE, "psE")
                    for ic in range(4):
                        kb.mm(ps[0][:], dbf[:, ic * 1024 + dc * 128: ic * 1024 + dc * 128 + 128],
                              hbf[:, ic * 2048 + blk * W: ic * 2048 + (blk + 1) * W],
                              ic == 0, ic == 3, reads=["dbf", "hbf"], writes=[ps[1]])
                    asl = acc[:, dc * 2048 + blk * W: dc * 2048 + (blk + 1) * W]
                    if e == 0:
                        kb.act(asl, ps[0][:], AF.Copy, reads=[ps[1]], writes=["acc"])
                    else:
                        kb.v("tensor_tensor", asl, asl, ps[0][:], ALU.add,
                             reads=["acc", ps[1]], writes=["acc"])
        for blk in range(4):
            for dc in range(8):
                kb.dma("sync", xs[:], XM[dc * 128:(dc + 1) * 128, blk * W:(blk + 1) * W],
                       writes=["xs"])
                kb.v("scalar_tensor_tensor", osb[:],
                     acc[:, dc * 2048 + blk * W: dc * 2048 + (blk + 1) * W], 0.5, xs[:],
                     ALU.mult, ALU.add, reads=["acc", "xs"], writes=["osb"])
                kb.dma("gpsimd", XO[dc * 128:(dc + 1) * 128, blk * W:(blk + 1) * W],
                       osb[:], reads=["osb"])
        kb.emit()
    return nc



def build_inproj():
    nc = bass.Bass("TRN2", target_bir_lowering=False)
    XI = nc.dram_tensor("XI", [1024, 2048], F32, kind="ExternalInput").ap()
    WIT = nc.dram_tensor("WIT", [1024, 4384], F32, kind="ExternalInput").ap()
    ZX = nc.dram_tensor("ZX", [4384, 2048], F32, kind="ExternalOutput").ap()
    with ExitStack() as ctx:
        kb = KB(nc, ctx)
        wst = _mk(nc, ctx, "wst", [128, 4384], F32)
        wbf = _mk(nc, ctx, "wbf", [128, 8 * 4384], BF16)
        xb = _mk(nc, ctx, "xb", [128, 8 * W], F32)
        lnb = _mk(nc, ctx, "lnb", [128, 8 * W], BF16)
        ev = _mk(nc, ctx, "ev", [128, W], F32)
        ev2 = _mk(nc, ctx, "ev2", [128, W], F32)
        aux, psA, psB, onesK, ones1 = _ln_aux(kb, nc, ctx)
        for kc in range(8):
            kb.dma("sync", wst[:], WIT[kc * 128:(kc + 1) * 128, :], writes=["wst"])
            kb.v("tensor_copy", wbf[:, kc * 4384:(kc + 1) * 4384], wst[:],
                 reads=["wst"], writes=["wbf"])
        oc_sizes = [(oc, 128) for oc in range(34)] + [(34, 32)]
        for blk in range(4):
            for kc in range(8):
                kb.dma("sync", xb[:, kc * W:(kc + 1) * W],
                       XI[kc * 128:(kc + 1) * 128, blk * W:(blk + 1) * W], writes=["xb"])
            emit_ln(kb, "l", 8,
                    lambda c: (xb[:, c * W:(c + 1) * W], "xb"),
                    lambda c: (lnb[:, c * W:(c + 1) * W], "lnb"), aux, 1024)
            for oc, sz in oc_sizes:
                ps, ek, ev_ = (psA, "psA", (ev, "ev")) if oc % 2 == 0 else (psB, "psB", (ev2, "ev2"))
                for kc in range(8):
                    kb.mm(ps[0:sz, :], wbf[:, kc * 4384 + oc * 128: kc * 4384 + oc * 128 + sz],
                          lnb[:, kc * W:(kc + 1) * W], kc == 0, kc == 7,
                          reads=["wbf", "lnb"], writes=[ek])
                kb.act(ev_[0][0:sz, :], ps[0:sz, :], AF.Copy, reads=[ek], writes=[ev_[1]])
                kb.dma("gpsimd", ZX[oc * 128: oc * 128 + sz, blk * W:(blk + 1) * W],
                       ev_[0][0:sz, :], reads=[ev_[1]])
        kb.emit()
    return nc



def build_conv():
    nc = bass.Bass("TRN2", target_bir_lowering=False)
    XBCE = nc.dram_tensor("XBCE", [2304, 2051], F32, kind="ExternalInput").ap()
    CW = nc.dram_tensor("CW", [2304, 4], F32, kind="ExternalInput").ap()
    CBI = nc.dram_tensor("CBI", [2304, 1], F32, kind="ExternalInput").ap()
    DTR = nc.dram_tensor("DTR", [32, 2048], F32, kind="ExternalInput").ap()
    DTB = nc.dram_tensor("DTB", [32, 1], F32, kind="ExternalInput").ap()
    ALOG = nc.dram_tensor("ALOG", [32, 1], F32, kind="ExternalInput").ap()
    XS = nc.dram_tensor("XS", [2304, 2048], F32, kind="ExternalOutput").ap()
    DT = nc.dram_tensor("DT", [32, 2048], F32, kind="ExternalOutput").ap()
    DA = nc.dram_tensor("DA", [32, 2048], F32, kind="ExternalOutput").ap()
    with ExitStack() as ctx:
        kb = KB(nc, ctx)
        ext = _mk(nc, ctx, "ext", [128, 2051], F32)
        accv = _mk(nc, ctx, "accv", [128, 2048], F32)
        cw = _mk(nc, ctx, "cw", [128, 18 * 4], F32)
        cb = _mk(nc, ctx, "cb", [128, 18], F32)
        dtr = _mk(nc, ctx, "dtr", [32, 2048], F32)
        dts = _mk(nc, ctx, "dts", [32, 2048], F32)
        das = _mk(nc, ctx, "das", [32, 2048], F32)
        dtb = _mk(nc, ctx, "dtb", [32, 1], F32)
        alog = _mk(nc, ctx, "alog", [32, 1], F32)
        nega = _mk(nc, ctx, "nega", [32, 1], F32)
        for ck in range(18):
            kb.dma("sync", cw[:, ck * 4:(ck + 1) * 4], CW[ck * 128:(ck + 1) * 128, :],
                   writes=["cw"])
            kb.dma("sync", cb[:, ck:ck + 1], CBI[ck * 128:(ck + 1) * 128, :], writes=["cb"])
        for ck in range(18):
            kb.dma("sync", ext[:], XBCE[ck * 128:(ck + 1) * 128, :], writes=["ext"])
            kb.v("tensor_scalar", accv[:], ext[:, 0:2048], cw[:, ck * 4:ck * 4 + 1],
                 None, ALU.mult, reads=["ext", "cw"], writes=["accv"])
            for k in range(1, 4):
                kb.v("scalar_tensor_tensor", accv[:], ext[:, k:k + 2048],
                     cw[:, ck * 4 + k:ck * 4 + k + 1], accv[:], ALU.mult, ALU.add,
                     reads=["ext", "cw", "accv"], writes=["accv"])
            kb.v("tensor_scalar", accv[:], accv[:], cb[:, ck:ck + 1], None, ALU.add,
                 reads=["accv", "cb"], writes=["accv"])
            kb.act(accv[:], accv[:], AF.Silu, reads=["accv"], writes=["accv"])
            kb.dma("gpsimd", XS[ck * 128:(ck + 1) * 128, :], accv[:], reads=["accv"])
        kb.dma("sync", dtr[:], DTR, writes=["dtr"])
        kb.dma("sync", dtb[:], DTB, writes=["dtb"])
        kb.dma("sync", alog[:], ALOG, writes=["alog"])
        kb.act(dts[:], dtr[:], AF.Sigmoid, bias=dtb[:], reads=["dtr", "dtb"], writes=["dts"])
        kb.v("tensor_scalar", dts[:], dts[:], 0.099, 0.001, ALU.mult, ALU.add,
             reads=["dts"], writes=["dts"])
        kb.act(nega[:], alog[:], AF.Exp, reads=["alog"], writes=["nega"])
        kb.v("tensor_scalar", nega[:], nega[:], -1.0, None, ALU.mult,
             reads=["nega"], writes=["nega"])
        kb.v("tensor_scalar", das[:], dts[:], nega[:], None, ALU.mult,
             reads=["dts", "nega"], writes=["das"])
        kb.dma("gpsimd", DT, dts[:], reads=["dts"])
        kb.dma("gpsimd", DA, das[:], reads=["das"])
        kb.emit()
    return nc



def build_ssd():
    nc = bass.Bass("TRN2", target_bir_lowering=False)
    XT = nc.dram_tensor("XT", [4096, 1024], F32, kind="ExternalInput").ap()
    BFD = nc.dram_tensor("BFD", [128, 4096], F32, kind="ExternalInput").ap()
    CFD = nc.dram_tensor("CFD", [128, 4096], F32, kind="ExternalInput").ap()
    BTD = nc.dram_tensor("BTD", [4096, 128], F32, kind="ExternalInput").ap()
    DTT = nc.dram_tensor("DTT", [4096, 16], F32, kind="ExternalInput").ap()
    DAT = nc.dram_tensor("DAT", [4096, 16], F32, kind="ExternalInput").ap()
    TRI = nc.dram_tensor("TRI", [64, 64], F32, kind="ExternalInput").ap()
    MBI = nc.dram_tensor("MBI", [64, 64], F32, kind="ExternalInput").ap()
    DEXP = nc.dram_tensor("DEXP", [1, 1024], F32, kind="ExternalInput").ap()
    YO = nc.dram_tensor("YO", [4096, 1024], F32, kind="ExternalOutput").ap()
    with ExitStack() as ctx:
        kb = KB(nc, ctx)
        stg = _mk(nc, ctx, "stg", [128, 4096], F32)
        bfb = _mk(nc, ctx, "bfb", [128, 4096], BF16)
        cfb = _mk(nc, ctx, "cfb", [128, 4096], BF16)
        btb = _mk(nc, ctx, "btb", [64, 64 * 128], BF16)
        btstg = _mk(nc, ctx, "btstg", [64, 64 * 128], F32)
        dtt = _mk(nc, ctx, "dtt", [64, 64 * 16], F32)
        dat = _mk(nc, ctx, "dat", [64, 64 * 16], F32)
        tri = _mk(nc, ctx, "tri", [64, 64], F32)
        mb = _mk(nc, ctx, "mb", [64, 64], F32)
        drow = _mk(nc, ctx, "drow", [1, 1024], F32)
        dexp = _mk(nc, ctx, "dexp", [64, 1024], F32)
        xc0 = _mk(nc, ctx, "xc0", [64, 1024], F32)
        xc1 = _mk(nc, ctx, "xc1", [64, 1024], F32)
        arow = _mk(nc, ctx, "arow", [1, 1024], F32)
        acum = _mk(nc, ctx, "acum", [64, 16], F32)
        eat = _mk(nc, ctx, "eat", [64, 16], F32)
        dtmp = _mk(nc, ctx, "dtmp", [64, 16], F32)
        dstb = _mk(nc, ctx, "dstb", [64, 16], BF16)
        cbt = _mk(nc, ctx, "cbt", [64, 64], F32)
        cdbc = _mk(nc, ctx, "cdbc", [128, 16], F32)
        csb = _mk(nc, ctx, "csb", [1, 16], F32)
        cdb = _mk(nc, ctx, "cdb", [1, 16], F32)
        xdt = _mk(nc, ctx, "xdt", [64, 1024], BF16)
        xds = _mk(nc, ctx, "xds", [64, 1024], BF16)
        e1 = _mk(nc, ctx, "e1", [64, 512], F32)
        mbf = _mk(nc, ctx, "mbf", [64, 1024], BF16)
        yos = _mk(nc, ctx, "yos", [64, 1024], F32)
        y1 = _mk(nc, ctx, "y1", [64, 1024], F32)
        t2 = _mk(nc, ctx, "t2", [64, 1024], F32)
        ysb0 = _mk(nc, ctx, "ysb0", [64, 1024], F32)
        ysb1 = _mk(nc, ctx, "ysb1", [64, 1024], F32)
        pall = _mk(nc, ctx, "pall", [128, 1024], F32)
        pbf = _mk(nc, ctx, "pbf", [128, 1024], BF16)
        ptmp = _mk(nc, ctx, "ptmp", [128, 1024], F32)
        ones1 = _mk(nc, ctx, "ones1", [1, 128], F32)
        ones64 = _mk(nc, ctx, "ones64", [64, 1], F32)
        psYd = _mkp(nc, ctx, "psYd", [64, 1024])
        psST = _mkp(nc, ctx, "psST", [128, 1024])
        psYo = _mkp(nc, ctx, "psYo", [64, 1024])
        psSm = _mkp(nc, ctx, "psSm", [128, 512])
        psAq = _mkp(nc, ctx, "psAq", [64, 512])
        kb.v("memset", ones1[:], 1.0, writes=["ones1"])
        kb.v("memset", ones64[:], 1.0, writes=["ones64"])
        kb.v("memset", pall[:], 0.0, writes=["pall"])
        kb.v("memset", pbf[:], 0.0, writes=["pbf"])
        kb.dma("sync", tri[:], TRI, writes=["tri"])
        kb.dma("sync", mb[:], MBI, writes=["mb"])
        kb.dma("sync", drow[:], DEXP, writes=["drow"])
        kb.dma("sync", stg[:], BFD, writes=["stg"])
        kb.v("tensor_copy", bfb[:], stg[:], reads=["stg"], writes=["bfb"])
        kb.dma("sync", stg[:], CFD, writes=["stg"])
        kb.v("tensor_copy", cfb[:], stg[:], reads=["stg"], writes=["cfb"])
        for part in range(4):
            kb.dma("sync",
                   btstg[:, part * 2048:(part + 1) * 2048].rearrange("q (c n) -> q c n", c=16),
                   BTD.rearrange("(c q) n -> q c n", q=64)[:, part * 16:(part + 1) * 16, :],
                   writes=["btstg"])
        kb.v("tensor_copy", btb[:], btstg[:], reads=["btstg"], writes=["btb"])
        for part in range(4):
            kb.dma("sync",
                   dtt[:, part * 256:(part + 1) * 256].rearrange("q (c h) -> q c h", c=16),
                   DTT.rearrange("(c q) h -> q c h", q=64)[:, part * 16:(part + 1) * 16, :],
                   writes=["dtt"])
            kb.dma("sync",
                   dat[:, part * 256:(part + 1) * 256].rearrange("q (c h) -> q c h", c=16),
                   DAT.rearrange("(c q) h -> q c h", q=64)[:, part * 16:(part + 1) * 16, :],
                   writes=["dat"])
        for half in range(2):
            kb.mm(psAq[:], ones1[0:1, 0:64], drow[0:1, half * 512:(half + 1) * 512],
                  True, True, reads=["ones1", "drow"], writes=["psAq"])
            kb.act(dexp[:, half * 512:(half + 1) * 512], psAq[:], AF.Copy,
                   reads=["psAq"], writes=["dexp"])
        for c in range(64):
            xc, kx = (xc0, "xc0") if c % 2 == 0 else (xc1, "xc1")
            ysb, ky = (ysb0, "ysb0") if c % 2 == 0 else (ysb1, "ysb1")
            dslice = dat[:, c * 16:(c + 1) * 16]
            tslice = dtt[:, c * 16:(c + 1) * 16]
            kb.dma("sync", xc[:], XT[c * 64:(c + 1) * 64, :], writes=[kx])
            kb.mm(psSm[0:64, 0:16], tri[:], dslice, True, True,
                  reads=["tri", "dat"], writes=["psSmA"])
            kb.act(acum[:], psSm[0:64, 0:16], AF.Copy, reads=["psSmA"], writes=["acum"])
            kb.act(eat[:], acum[:], AF.Exp, reads=["acum"], writes=["eat"])
            kb.mm(psSm[0:1, 16:32], ones64[:], dslice, True, True,
                  reads=["ones64", "dat"], writes=["psSmB"])
            kb.act(csb[:], psSm[0:1, 16:32], AF.Copy, reads=["psSmB"], writes=["csb"])
            kb.act(cdb[:], csb[:], AF.Exp, reads=["csb"], writes=["cdb"])
            kb.mm(psSm[0:64, 32:48], ones1[0:1, 0:64], csb[:], True, True,
                  reads=["ones1", "csb"], writes=["psSmC"])
            kb.v("tensor_tensor", dtmp[:], psSm[0:64, 32:48], acum[:], ALU.subtract,
                 reads=["psSmC", "acum"], writes=["dtmp"])
            kb.act(dstb[:], dtmp[:], AF.Exp, reads=["dtmp"], writes=["dstb"])
            kb.mm(psSm[0:128, 48:64], ones1[0:1, 0:128], cdb[:], True, True,
                  reads=["ones1", "cdb"], writes=["psSmD"])
            kb.act(cdbc[:], psSm[0:128, 48:64], AF.Copy, reads=["psSmD"], writes=["cdbc"])
            kb.dma("gpsimd", arow[0:1, :].rearrange("o (q h) -> o q h", q=64), acum[:],
                   reads=["acum"], writes=["arow"])
            kb.v("tensor_tensor", xdt[:].rearrange("t (h p) -> t h p", h=16),
                 xc[:].rearrange("t (h p) -> t h p", h=16),
                 tslice.unsqueeze(2).broadcast_to([64, 16, 64]), ALU.mult,
                 reads=[kx, "dtt"], writes=["xdt"])
            kb.v("tensor_tensor", xds[:].rearrange("t (h p) -> t h p", h=16),
                 xdt[:].rearrange("t (h p) -> t h p", h=16),
                 dstb[:].unsqueeze(2).broadcast_to([64, 16, 64]), ALU.mult,
                 reads=["xdt", "dstb"], writes=["xds"])
            kb.mm(psSm[0:64, 64:128], bfb[:, c * 64:(c + 1) * 64],
                  cfb[:, c * 64:(c + 1) * 64], True, True,
                  reads=["bfb", "cfb"], writes=["psSmE"])
            kb.act(cbt[:], psSm[0:64, 64:128], AF.Copy, reads=["psSmE"], writes=["cbt"])
            for half in range(2):
                hs = half * 8
                arv = arow[0:1, :].rearrange("o (q h) -> o h q", q=64)[:, hs:hs + 8, :]
                kb.mm(psAq[:], ones1[0:1, 0:64], arv, True, True,
                      reads=["ones1", "arow"], writes=["psAq"])
                kb.v("tensor_tensor", e1[:].rearrange("s (h q) -> s h q", h=8), psAq[:].rearrange("s (h q) -> s h q", h=8),
                     acum[:, hs:hs + 8].unsqueeze(2).broadcast_to([64, 8, 64]),
                     ALU.subtract, reads=["psAq", "acum"], writes=["e1"])
                kb.v("tensor_tensor", e1[:].rearrange("s (h q) -> s h q", h=8),
                     e1[:].rearrange("s (h q) -> s h q", h=8),
                     mb[:].unsqueeze(1).broadcast_to([64, 8, 64]), ALU.add,
                     reads=["e1", "mb"], writes=["e1"])
                kb.act(e1[:], e1[:], AF.Exp, reads=["e1"], writes=["e1"])
                kb.v("tensor_tensor", mbf[:, half * 512:(half + 1) * 512].rearrange("s (h q) -> s h q", h=8),
                     e1[:].rearrange("s (h q) -> s h q", h=8),
                     cbt[:].unsqueeze(1).broadcast_to([64, 8, 64]), ALU.mult,
                     reads=["e1", "cbt"], writes=["mbf"])
            for h in range(16):
                kb.mm(psYd[0:64, h * 64:(h + 1) * 64], mbf[:, h * 64:(h + 1) * 64],
                      xdt[:, h * 64:(h + 1) * 64], True, True,
                      reads=["mbf", "xdt"], writes=["psYd"])
                kb.mm(psST[0:128, h * 64:(h + 1) * 64],
                      btb[:, c * 128:(c + 1) * 128],
                      xds[:, h * 64:(h + 1) * 64], True, True,
                      reads=["btb", "xds"], writes=["psST"])
                kb.mm(psYo[0:64, h * 64:(h + 1) * 64], cfb[:, c * 64:(c + 1) * 64],
                      pbf[:, h * 64:(h + 1) * 64], True, True,
                      reads=["cfb", "pbf"], writes=["psYo"])
            kb.v("tensor_tensor", yos[:].rearrange("t (h p) -> t h p", h=16),
                 psYo[:].rearrange("t (h p) -> t h p", h=16),
                 eat[:].unsqueeze(2).broadcast_to([64, 16, 64]), ALU.mult,
                 reads=["psYo", "eat"], writes=["yos"])
            kb.v("tensor_tensor", y1[:], psYd[:], yos[:], ALU.add,
                 reads=["psYd", "yos"], writes=["y1"])
            kb.v("tensor_tensor", t2[:], xc[:], dexp[:], ALU.mult,
                 reads=[kx, "dexp"], writes=["t2"])
            kb.v("tensor_tensor", ysb[:], t2[:], y1[:], ALU.add,
                 reads=["t2", "y1"], writes=[ky])
            kb.dma("gpsimd", YO[c * 64:(c + 1) * 64, :], ysb[:], reads=[ky])
            kb.v("tensor_tensor", ptmp[:].rearrange("n (h p) -> n h p", h=16),
                 pall[:].rearrange("n (h p) -> n h p", h=16),
                 cdbc[:].unsqueeze(2).broadcast_to([128, 16, 64]), ALU.mult,
                 reads=["pall", "cdbc"], writes=["ptmp"])
            kb.v("tensor_tensor", pall[:], ptmp[:], psST[:], ALU.add,
                 reads=["ptmp", "psST"], writes=["pall"])
            kb.v("tensor_copy", pbf[:], pall[:], reads=["pall"], writes=["pbf"])
        kb.emit()
    return nc



def build_gateout():
    nc = bass.Bass("TRN2", target_bir_lowering=False)
    YF = nc.dram_tensor("YF", [2048, 2048], F32, kind="ExternalInput").ap()
    ZF = nc.dram_tensor("ZF", [2048, 2048], F32, kind="ExternalInput").ap()
    XG = nc.dram_tensor("XG", [1024, 2048], F32, kind="ExternalInput").ap()
    OPT = nc.dram_tensor("OPT", [2048, 1024], F32, kind="ExternalInput").ap()
    NW = nc.dram_tensor("NW", [2048, 1], F32, kind="ExternalInput").ap()
    LWI = nc.dram_tensor("LWI", [1024, 1], F32, kind="ExternalInput").ap()
    LBI = nc.dram_tensor("LBI", [1024, 1], F32, kind="ExternalInput").ap()
    XO = nc.dram_tensor("XO", [1024, 2048], F32, kind="ExternalOutput").ap()
    with ExitStack() as ctx:
        kb = KB(nc, ctx)
        opt = _mk(nc, ctx, "opt", [128, 16 * 1024], BF16)
        wstg = _mk(nc, ctx, "wstg", [128, 1024], F32)
        yb = _mk(nc, ctx, "yb", [128, 16 * W], F32)
        zst = _mk(nc, ctx, "zst", [128, W], F32)
        ynb = _mk(nc, ctx, "ynb", [128, 16 * W], BF16)
        ynt = _mk(nc, ctx, "ynt", [128, W], F32)
        proj = _mk(nc, ctx, "proj", [128, 8 * W], F32)
        obuf = _mk(nc, ctx, "obuf", [128, 8 * W], F32)
        xgb = _mk(nc, ctx, "xgb", [128, 8 * W], F32)
        rbc = _mk(nc, ctx, "rbc", [128, W], F32)
        nw = _mk(nc, ctx, "nw", [128, 16], F32)
        lw = _mk(nc, ctx, "lw", [128, 8], F32)
        lb = _mk(nc, ctx, "lb", [128, 8], F32)
        aux, psA, psB, onesK, ones1 = _ln_aux(kb, nc, ctx)
        psP0 = _mkp(nc, ctx, "psP0", [128, W])
        psP1 = _mkp(nc, ctx, "psP1", [128, W])
        for ic in range(16):
            kb.dma("sync", wstg[:], OPT[ic * 128:(ic + 1) * 128, :], writes=["wstg"])
            kb.v("tensor_copy", opt[:, ic * 1024:(ic + 1) * 1024], wstg[:],
                 reads=["wstg"], writes=["opt"])
            kb.dma("sync", nw[:, ic:ic + 1], NW[ic * 128:(ic + 1) * 128, :], writes=["nw"])
        for dc in range(8):
            kb.dma("sync", lw[:, dc:dc + 1], LWI[dc * 128:(dc + 1) * 128, :], writes=["lw"])
            kb.dma("sync", lb[:, dc:dc + 1], LBI[dc * 128:(dc + 1) * 128, :], writes=["lb"])
        for blk in range(4):
            for ic in range(16):
                ysl = yb[:, ic * W:(ic + 1) * W]
                kb.dma("sync", ysl, YF[ic * 128:(ic + 1) * 128, blk * W:(blk + 1) * W],
                       writes=["yb"])
                kb.dma("sync", zst[:], ZF[ic * 128:(ic + 1) * 128, blk * W:(blk + 1) * W],
                       writes=["zst"])
                kb.act(zst[:], zst[:], AF.Silu, reads=["zst"], writes=["zst"])
                kb.v("tensor_tensor", ysl, ysl, zst[:], ALU.mult,
                     reads=["yb", "zst"], writes=["yb"])
            sq, ksq = aux["sq"]
            rows, krows = aux["rows"]
            for ic in range(16):
                kb.act(sq, yb[:, ic * W:(ic + 1) * W], AF.Square, reads=["yb"], writes=[ksq])
                kb.mm(aux["psS"][0], onesK[:], sq, ic == 0, ic == 15,
                      reads=[ksq, "onesK"], writes=["psS"])
            kb.v("tensor_scalar", rows[0:1, 0:W], aux["psS"][0], 1.0 / 2048, 1e-5,
                 ALU.mult, ALU.add, reads=["psS"], writes=[krows])
            kb.act(rows[0:1, W:2 * W], rows[0:1, 0:W], AF.Sqrt,
                   reads=[krows], writes=[krows])
            kb.v("reciprocal", rows[0:1, 2 * W:3 * W], rows[0:1, W:2 * W],
                 reads=[krows], writes=[krows])
            kb.mm(psA[:], ones1[:], rows[0:1, 2 * W:3 * W], True, True,
                  reads=["ones1", krows], writes=["psA"])
            kb.act(rbc[:], psA[:], AF.Copy, reads=["psA"], writes=["rbc"])
            for ic in range(16):
                kb.v("tensor_tensor", ynt[:], yb[:, ic * W:(ic + 1) * W], rbc[:], ALU.mult,
                     reads=["yb", "rbc"], writes=["ynt"])
                kb.v("tensor_scalar", ynb[:, ic * W:(ic + 1) * W], ynt[:], nw[:, ic:ic + 1],
                     None, ALU.mult, reads=["ynt", "nw"], writes=["ynb"])
            for dc in range(8):
                ps = (psP0, "psP0") if dc % 2 == 0 else (psP1, "psP1")
                for ic in range(16):
                    kb.mm(ps[0][:], opt[:, ic * 1024 + dc * 128: ic * 1024 + dc * 128 + 128],
                          ynb[:, ic * W:(ic + 1) * W], ic == 0, ic == 15,
                          reads=["opt", "ynb"], writes=[ps[1]])
                kb.act(proj[:, dc * W:(dc + 1) * W], ps[0][:], AF.Copy,
                       reads=[ps[1]], writes=["proj"])
            emit_ln(kb, "p", 8,
                    lambda c: (proj[:, c * W:(c + 1) * W], "proj"),
                    lambda c: (obuf[:, c * W:(c + 1) * W], "obuf"), aux, 1024)
            for dc in range(8):
                kb.v("tensor_scalar", obuf[:, dc * W:(dc + 1) * W],
                     obuf[:, dc * W:(dc + 1) * W], lw[:, dc:dc + 1], lb[:, dc:dc + 1],
                     ALU.mult, ALU.add, reads=["obuf", "lw", "lb"], writes=["obuf"])
            for kc in range(8):
                kb.dma("sync", xgb[:, kc * W:(kc + 1) * W],
                       XG[kc * 128:(kc + 1) * 128, blk * W:(blk + 1) * W], writes=["xgb"])
            emit_ln(kb, "x", 8,
                    lambda c: (xgb[:, c * W:(c + 1) * W], "xgb"),
                    lambda c: (yb[:, c * W:(c + 1) * W], "yb"), aux, 1024)
            for dc in range(8):
                osl = obuf[:, dc * W:(dc + 1) * W]
                kb.v("tensor_tensor", osl, osl, yb[:, dc * W:(dc + 1) * W], ALU.add,
                     reads=["obuf", "yb"], writes=["obuf"])
                kb.v("tensor_tensor", osl, osl, xgb[:, dc * W:(dc + 1) * W], ALU.add,
                     reads=["obuf", "xgb"], writes=["obuf"])
                kb.dma("gpsimd", XO[dc * 128:(dc + 1) * 128, blk * W:(blk + 1) * W], osl,
                       reads=["obuf"])
        kb.emit()
    return nc



_prog_cache = {}


def _prog(name, builder):
    if name not in _prog_cache:
        _prog_cache[name] = builder()
    return _prog_cache[name]


def _run(name, builder, in_maps, out_names):
    nc = _prog(name, builder)
    res = run_bass_kernel_spmd(nc, in_maps, core_ids=list(range(NCORE)))
    return [{k: r[k] for k in out_names} for r in res.results]


def _f32(a):
    return np.ascontiguousarray(np.asarray(a), dtype=np.float32)


def _launch_ffn0(xT, WRT, WKT, WVT, MUR, MUK):
    ims = []
    for c in range(NCORE):
        prev = np.zeros((1024, 512), np.float32) if c % 2 == 0 \
            else np.ascontiguousarray(xT[:, c * TPC - 512: c * TPC])
        xe = np.concatenate([prev, xT[:, c * TPC:(c + 1) * TPC]], axis=1)
        ims.append(dict(XE=np.ascontiguousarray(xe), WRT=WRT, WKT=WKT, WVT=WVT,
                        MUR=MUR, MUK=MUK))
    rs = _run("ffn0", build_ffn0, ims, ["XO"])
    return np.concatenate([r["XO"] for r in rs], axis=1)


def _launch_attn(xT, W_qkv, b_qkv, A_qkv, B_qkv):
    id_bf = np.eye(128).astype(ml_dtypes.bfloat16)
    ims = []
    for c in range(NCORE):
        b, j = c // 2, c % 2
        heads = range(8 * j, 8 * j + 8)
        idx = np.array([s * 1024 + h * 64 + d for s in range(3) for h in heads
                        for d in range(64)])
        ims.append(dict(XB=np.ascontiguousarray(xT[:, b * 4096:(b + 1) * 4096]),
                        WSL=np.ascontiguousarray(W_qkv[:, idx]),
                        BSL=np.ascontiguousarray(b_qkv[idx].reshape(-1, 1)),
                        AQ=A_qkv,
                        BQSL=np.ascontiguousarray(B_qkv[:, idx]),
                        IDB=id_bf))
    rs = _run("attn", build_attn, ims, ["AO"])
    attn = np.empty((1024, 16384), np.float32)
    for c in range(NCORE):
        b, j = c // 2, c % 2
        attn[j * 512:(j + 1) * 512, b * 4096:(b + 1) * 4096] = rs[c]["AO"]
    return attn


def _launch_projres(attn, xT, W_proj, b_proj, A_proj, B_proj):
    ims = []
    for c in range(NCORE):
        s = slice(c * TPC, (c + 1) * TPC)
        ims.append(dict(AT=np.ascontiguousarray(attn[:, s]),
                        XR=np.ascontiguousarray(xT[:, s]),
                        WP=W_proj, BP=b_proj.reshape(-1, 1),
                        APJ=A_proj, BPJ=B_proj))
    rs = _run("projres", build_projres, ims, ["XO"])
    return np.concatenate([r["XO"] for r in rs], axis=1)


def _launch_moe(xT, gate_w, bias_e, up_w, down_w, shared_up, shared_down):
    id_f = np.eye(128, dtype=np.float32)
    gwt = np.ascontiguousarray(gate_w.T)
    bit = np.ascontiguousarray(np.broadcast_to(bias_e, (128, 16)))
    ltm = (np.arange(16)[None, :] < np.arange(16)[:, None]).astype(np.float32)
    ltm = np.ascontiguousarray(np.broadcast_to(ltm.reshape(1, 256), (128, 256)))
    eup = np.concatenate([up_w.transpose(0, 2, 1),
                          shared_up.reshape(4, 512, 1024).transpose(0, 2, 1)], axis=0)
    edn = np.concatenate([down_w.transpose(0, 2, 1),
                          np.ascontiguousarray(shared_down.T).reshape(4, 512, 1024)], axis=0)
    eup = np.ascontiguousarray(eup)
    edn = np.ascontiguousarray(edn)
    ims = []
    for c in range(NCORE):
        s = slice(c * TPC, (c + 1) * TPC)
        ims.append(dict(XM=np.ascontiguousarray(xT[:, s]), GWT=gwt, BIT=bit, LTM=ltm,
                        EUP=eup, EDN=edn, IDF=id_f))
    rs = _run("moe", build_moe, ims, ["XO"])
    return np.concatenate([r["XO"] for r in rs], axis=1)


def _launch_inproj(xT, in_proj_w):
    wit = np.ascontiguousarray(in_proj_w.T)
    ims = []
    for c in range(NCORE):
        s = slice(c * TPC, (c + 1) * TPC)
        ims.append(dict(XI=np.ascontiguousarray(xT[:, s]), WIT=wit))
    rs = _run("inproj", build_inproj, ims, ["ZX"])
    return np.concatenate([r["ZX"] for r in rs], axis=1)


def _launch_conv(xbc, dtr, conv_w, conv_b, dt_bias, A_log):
    ims = []
    for c in range(NCORE):
        s = slice(c * TPC, (c + 1) * TPC)
        prev = np.zeros((2304, 3), np.float32) if c % 2 == 0 \
            else np.ascontiguousarray(xbc[:, c * TPC - 3: c * TPC])
        xe = np.concatenate([prev, xbc[:, s]], axis=1)
        ims.append(dict(XBCE=np.ascontiguousarray(xe), CW=conv_w,
                        CBI=conv_b.reshape(-1, 1),
                        DTR=np.ascontiguousarray(dtr[:, s]),
                        DTB=dt_bias.reshape(-1, 1), ALOG=A_log.reshape(-1, 1)))
    rs = _run("conv", build_conv, ims, ["XS", "DT", "DA"])
    xs = np.concatenate([r["XS"] for r in rs], axis=1)
    dt = np.concatenate([r["DT"] for r in rs], axis=1)
    da = np.concatenate([r["DA"] for r in rs], axis=1)
    return xs, dt, da


def _launch_ssd(xs, dt, da, D_skip):
    xssm_tok = np.ascontiguousarray(xs[0:2048].T)
    b_fm = xs[2048:2176]
    c_fm = xs[2176:2304]
    b_tok = np.ascontiguousarray(b_fm.T)
    dtT = np.ascontiguousarray(dt.T)
    daT = np.ascontiguousarray(da.T)
    tri = np.ascontiguousarray(np.triu(np.ones((64, 64), np.float32)))
    mbi = np.where(np.triu(np.ones((64, 64), bool)), 0.0, -1e30).astype(np.float32)
    ims = []
    for c in range(NCORE):
        b, jh = c // 2, c % 2
        t = slice(b * 4096, (b + 1) * 4096)
        ims.append(dict(XT=np.ascontiguousarray(xssm_tok[t, jh * 1024:(jh + 1) * 1024]),
                        BFD=np.ascontiguousarray(b_fm[:, t]),
                        CFD=np.ascontiguousarray(c_fm[:, t]),
                        BTD=np.ascontiguousarray(b_tok[t, :]),
                        DTT=np.ascontiguousarray(dtT[t, 16 * jh: 16 * jh + 16]),
                        DAT=np.ascontiguousarray(daT[t, 16 * jh: 16 * jh + 16]),
                        TRI=tri, MBI=mbi,
                        DEXP=np.ascontiguousarray(
                            np.repeat(D_skip[16 * jh: 16 * jh + 16], 64).reshape(1, 1024))))
    rs = _run("ssd", build_ssd, ims, ["YO"])
    y_tok = np.empty((16384, 2048), np.float32)
    for c in range(NCORE):
        b, jh = c // 2, c % 2
        y_tok[b * 4096:(b + 1) * 4096, jh * 1024:(jh + 1) * 1024] = rs[c]["YO"]
    return np.ascontiguousarray(y_tok.T)


def _launch_gateout(y_fm, z_fm, xT, out_proj_w, norm_w, ln_w, ln_b):
    opt = np.ascontiguousarray(out_proj_w.T)
    ims = []
    for c in range(NCORE):
        s = slice(c * TPC, (c + 1) * TPC)
        ims.append(dict(YF=np.ascontiguousarray(y_fm[:, s]),
                        ZF=np.ascontiguousarray(z_fm[:, s]),
                        XG=np.ascontiguousarray(xT[:, s]),
                        OPT=opt, NW=norm_w.reshape(-1, 1),
                        LWI=ln_w.reshape(-1, 1), LBI=ln_b.reshape(-1, 1)))
    rs = _run("gateout", build_gateout, ims, ["XO"])
    return np.concatenate([r["XO"] for r in rs], axis=1)


def kernel(**inp):
    g = {k: _f32(v) for k, v in inp.items()}
    x = g["L0_attn_x"].reshape(16384, 1024)
    xT = np.ascontiguousarray(x.T)

    wrt = np.ascontiguousarray(g["L0_ffn_W_r"].T)
    wkt = np.ascontiguousarray(g["L0_ffn_W_k"].T)
    wvt = np.ascontiguousarray(g["L0_ffn_W_v"].T)
    mur = g["L0_ffn_mu_r"].reshape(-1, 1)
    muk = g["L0_ffn_mu_k"].reshape(-1, 1)

    xT = _launch_ffn0(xT, wrt, wkt, wvt, mur, muk)
    attn = _launch_attn(xT, g["L0_attn_W_qkv"], g["L0_attn_b_qkv"],
                        g["L0_attn_A_qkv"], g["L0_attn_B_qkv"])
    xT = _launch_projres(attn, xT, g["L0_attn_W_proj"], g["L0_attn_b_proj"],
                         g["L0_attn_A_proj"], g["L0_attn_B_proj"])
    xT = _launch_ffn0(xT, wrt, wkt, wvt, mur, muk)

    xT = _launch_moe(xT, g["L1_ffn_gate_w"], g["L1_ffn_bias_e"], g["L1_ffn_up_w"],
                     g["L1_ffn_down_w"], g["L1_ffn_shared_up"], g["L1_ffn_shared_down"])
    zx = _launch_inproj(xT, g["L1_attn_in_proj_w"])
    z_fm = zx[0:2048]
    xbc = zx[2048:4352]
    dtr = zx[4352:4384]
    xs, dt, da = _launch_conv(xbc, dtr, g["L1_attn_conv_w"], g["L1_attn_conv_b"],
                              g["L1_attn_dt_bias"], g["L1_attn_A_log"])
    y_fm = _launch_ssd(xs, dt, da, g["L1_attn_D_skip"])
    xT = _launch_gateout(y_fm, z_fm, xT, g["L1_attn_out_proj_w"], g["L1_attn_norm_w"],
                         g["L1_attn_ln_w"], g["L1_attn_ln_b"])
    xT = _launch_moe(xT, g["L1_ffn_gate_w"], g["L1_ffn_bias_e"], g["L1_ffn_up_w"],
                     g["L1_ffn_down_w"], g["L1_ffn_shared_up"], g["L1_ffn_shared_down"])

    return np.ascontiguousarray(xT.T).reshape(4, 4096, 1024)
